# version 06
# baseline (speedup 1.0000x reference)
import math
import os

import numpy as np
import ml_dtypes
import concourse.bass as bass
import concourse.mybir as mybir
from concourse.bass_utils import run_bass_kernel_spmd

F32 = mybir.dt.float32
BF16 = mybir.dt.bfloat16
AF = mybir.ActivationFunctionType
ALU = mybir.AluOpType
AX = mybir.AxisListType

S = 2048
DM = 2048
NH = 16
HD = 128
DH = 8192
NDC = DM // 128
NHC = DH // 128
NEC = DM // 128
SCALE = 1.0 / math.sqrt(HD)
NEG = -1.0e30
EPS = 1.0e-5
SP = 512
NPASS = S // SP

DBG = bool(os.environ.get("KDBG"))

ENG_SEM = {"t": "pe", "v": "dve", "s": "act", "g": "gw", "y": "io"}


class R:
    __slots__ = ("wr", "rd")

    def __init__(self):
        self.wr = None
        self.rd = {}


class P:
    def __init__(self, nc):
        self.nc = nc
        self.ops = {"t": [], "v": [], "s": [], "g": [], "y": []}
        self.semh = {}
        self.cnts = {}
        self.hw = {}

    def sem(self, name, handle):
        self.semh[name] = handle
        self.cnts[name] = 0

    def cnt(self, n):
        return self.cnts[n]

    def wait(self, eng, sem, val):
        if val <= 0 or ENG_SEM[eng] == sem:
            return
        k = (eng, sem)
        if self.hw.get(k, 0) >= val:
            return
        self.hw[k] = val
        h = self.semh[sem]
        self.ops[eng].append(lambda e, h=h, v=val: e.wait_ge(h, v))

    def op(self, eng, fn, inc=False):
        if not inc:
            self.ops[eng].append(fn)
            return None
        sem = ENG_SEM[eng]
        step = 16 if sem in ("gw", "io") else 1
        self.cnts[sem] += step
        v = self.cnts[sem]
        h = self.semh[sem]
        self.ops[eng].append(lambda e, fn=fn, h=h, s=step: fn(e).then_inc(h, s))
        return v

    def gbar(self):
        v = self.cnts["gw"]
        h = self.semh["gw"]
        self.ops["g"].append(lambda e, h=h, v=v: e.wait_ge(h, v))

    def use(self, eng, *rs):
        for r in rs:
            if r.wr:
                self.wait(eng, r.wr[0], r.wr[1])

    def rd_done(self, sem, cnt, *rs):
        for r in rs:
            if r.rd.get(sem, 0) < cnt:
                r.rd[sem] = cnt

    def wr_acq(self, eng, *rs):
        for r in rs:
            if r.wr:
                self.wait(eng, r.wr[0], r.wr[1])
            for s, c in r.rd.items():
                self.wait(eng, s, c)
            r.rd = {}

    def wr_done(self, sem, cnt, *rs):
        for r in rs:
            r.wr = (sem, cnt)


def ap3(handle, rowlen, offset, step, n, width):
    return bass.AP(handle, offset, [[rowlen, 128], [step, n], [1, width]])


BASE = 16512
OFF_XN = BASE + 0
OFF_CTX = BASE + 65536
OFF_V4 = BASE + 131072
OFF_KQ = BASE + 147456
OFF_WQK = BASE + 163840
OFF_PROBS = BASE + 180224
OFF_PROBST = BASE + 188416
OFF_MASK = BASE + 196608
OFF_BVBC = BASE + 198656
OFF_T1 = BASE + 200704
OFF_IDENT = BASE + 204800
OFF_ONESB = BASE + 205056
OFF_ONESF = BASE + 205312
OFF_DTI = BASE + 205824
OFF_VEC = BASE + 207872
OFF_BQ = BASE + 208064
OFF_BK = BASE + 208128
OFF_BP = BASE + 208192
OFF_DS = BASE + 208256
OFF_DSOS = BASE + 208320
OFF_OSC = BASE + 208384
OFF_US = BASE + 208448
SB_CAP = 229344
assert OFF_US + 256 <= SB_CAP



def build_nc():
    nc = bass.Bass("TRN2", target_bir_lowering=False)

    di = {}
    def inp(name, shape, dt=F32):
        di[name] = nc.dram_tensor(name, shape, dt, kind="ExternalInput").ap()
    inp("x", [S, DM]); inp("xT", [DM, S])
    inp("wqT", [DM, DM]); inp("wkT", [DM, DM]); inp("wvT", [DM, DM])
    inp("bq2d", [128, NH]); inp("bk2d", [128, NH]); inp("bvrow", [1, DM])
    inp("wpT", [DM, DM]); inp("bp2d", [128, NEC])
    inp("upT", [DM, DH]); inp("dnT", [DH, DM])
    inp("us2d", [128, NHC]); inp("ds2d", [128, NEC]); inp("osc", [1, 1])
    inp("maskc", [128, 512]); inp("identc", [128, 128], BF16)
    inp("onesb", [1, 128], BF16); inp("onesf", [1, 128])
    y = nc.dram_tensor("y", [DM, S], F32, kind="ExternalOutput").ap()
    up_s = nc.dram_tensor("up_s", [DH, DM], BF16, kind="Internal").ap()
    dn_s = nc.dram_tensor("dn_s", [DM, DH], BF16, kind="Internal").ap()
    pj_s = nc.dram_tensor("pj_s", [DM, DM], BF16, kind="Internal").ap()
    dbg = {}
    if DBG:
        for nm, w in [("xn", 32768), ("ctx", 32768), ("hid", 16384),
                      ("kqv", 16384), ("pt", 4096)]:
            dbg[nm] = nc.dram_tensor("dbg_" + nm, [128, w], BF16,
                                     kind="ExternalOutput").ap()
        for nm, w in [("misc", 80), ("t1", 1024), ("vecI", 48), ("vecA", 48),
                      ("vecB", 48), ("dsosI", 16)]:
            dbg[nm] = nc.dram_tensor("dbg_" + nm, [128, w], F32,
                                     kind="ExternalOutput").ap()
        for nm, w in [("dti", 1024), ("probs", 4096)]:
            dbg[nm] = nc.dram_tensor("dbg_" + nm, [128, w], BF16,
                                     kind="ExternalOutput").ap()
        dbg["dnw"] = nc.dram_tensor("dbg_dnw", [128, 8192], BF16,
                                    kind="ExternalOutput").ap()
        dbg["hid2"] = nc.dram_tensor("dbg_hid2", [128, 32768], BF16,
                                     kind="ExternalOutput").ap()

    A = lambda name, shape, dt, off: nc.alloc_sbuf_tensor_at(name, shape, dt, offset=off)
    XN = A("XN", [128, 32768], BF16, OFF_XN)
    CTX = A("CTX", [128, 32768], BF16, OFF_CTX)
    V4 = A("V4", [128, 8192], BF16, OFF_V4)
    KQ = A("KQ", [128, 8192], BF16, OFF_KQ)
    WQK = A("WQK", [128, 8192], BF16, OFF_WQK)
    DNW = A("DNW", [128, 16384], BF16, OFF_V4)
    UPW = A("UPW", [128, 4096], BF16, OFF_WQK)
    PJW = A("PJW", [128, 4096], BF16, OFF_WQK + 8192)
    XNT = A("XNT", [128, 2048], BF16, OFF_WQK + 8192)
    PROBS = A("PROBS", [128, 4096], BF16, OFF_PROBS)
    PROBST = A("PROBST", [128, 4096], BF16, OFF_PROBST)
    XNS = A("XNS", [128, 8192], BF16, OFF_PROBS)
    MASK = A("MASK", [128, 512], F32, OFF_MASK)
    BVBC = A("BVBC", [128, 512], BF16, OFF_BVBC)
    BVROW = A("BVROW", [1, 512], BF16, OFF_BVBC + 1024)
    XRES0 = A("XRES0", [128, 512], F32, OFF_MASK)
    XRES1 = A("XRES1", [128, 512], F32, OFF_BVBC)
    T1_0 = A("T1_0", [128, 512], F32, OFF_T1)
    T1_1 = A("T1_1", [128, 512], F32, OFF_T1 + 2048)
    XTLN = A("XTLN", [128, 2048], F32, OFF_V4)
    IDENT = A("IDENT", [128, 128], BF16, OFF_IDENT)
    ONESB = A("ONESB", [1, 128], BF16, OFF_ONESB)
    ONESF = A("ONESF", [1, 128], F32, OFF_ONESF)
    DTI = A("DTI", [128, 1024], BF16, OFF_DTI)
    VEC = A("VEC", [128, 48], F32, OFF_VEC)
    BQ = A("BQ", [128, NH], F32, OFF_BQ)
    BK = A("BK", [128, NH], F32, OFF_BK)
    BP = A("BP", [128, NEC], F32, OFF_BP)
    DS = A("DS", [128, NEC], F32, OFF_DS)
    DSOS = A("DSOS", [128, NEC], F32, OFF_DSOS)
    OSC = A("OSC", [1, 1], F32, OFF_OSC)
    US = A("US", [128, NHC], F32, OFF_US)

    PS_A = nc.alloc_psum_tensor("PS_A", [128, 1024], F32)
    PS_B = nc.alloc_psum_tensor("PS_B", [128, 1024], F32)
    PS_C0 = nc.alloc_psum_tensor("PS_C0", [128, 512], F32)
    PS_C1 = nc.alloc_psum_tensor("PS_C1", [128, 512], F32)
    PS_K0 = nc.alloc_psum_tensor("PS_K0", [128, 512], F32)
    PS_K1 = nc.alloc_psum_tensor("PS_K1", [128, 512], F32)
    PB_AP = [PS_A[:, 0:512], PS_A[:, 512:1024], PS_B[:, 0:512], PS_B[:, 512:1024],
             PS_C0[:, :], PS_C1[:, :], PS_K0[:, :], PS_K1[:, :]]
    PB_H = [PS_A, PS_A, PS_B, PS_B, PS_C0, PS_C1, PS_K0, PS_K1]
    PB_OF = [0, 512, 0, 512, 0, 0, 0, 0]
    PB_RL = [1024, 1024, 1024, 1024, 512, 512, 512, 512]

    p = P(nc)
    import contextlib
    stack = contextlib.ExitStack()
    for sn in ["pe", "dve", "act", "gw", "io"]:
        p.sem(sn, stack.enter_context(nc.semaphore("sem_" + sn)))

    PB = [R() for _ in range(8)]
    R_XN = R(); R_CTX = R()
    R_A = R()
    R_B = [R(), R()]
    R_C0 = [R(), R()]
    R_C1 = [R(), R()]
    R_PR = [R(), R()]
    R_PT = R()
    R_MK = R()
    R_BV = R()
    R_T1 = [R(), R()]
    R_DTI = [R(), R()]
    R_CONST = R()
    R_VECLN = R()

    for name, dst in [("maskc", MASK), ("identc", IDENT), ("onesb", ONESB),
                      ("onesf", ONESF), ("bq2d", BQ), ("bk2d", BK),
                      ("bp2d", BP), ("ds2d", DS), ("osc", OSC), ("us2d", US)]:
        v = p.op("y", lambda e, d=dst, s=di[name]: e.dma_start(d[:, :], s), inc=True)
    p.wr_done("io", v, R_CONST, R_MK)

    p.use("t", R_CONST)
    p.wr_acq("t", PB[7])
    v = p.op("t", lambda e: e.matmul(PB_AP[7][:, 0:1], ONESF[:, :], OSC[:, :],
                                     start=True, stop=True), inc=True)
    p.wr_done("pe", v, PB[7])
    p.use("v", PB[7]); p.use("v", R_CONST)
    vc = p.op("v", lambda e: e.tensor_copy(VEC[:, 40:41], PB_AP[7][:, 0:1]),
              inc=True)
    p.ops["v"].append(lambda e, h=p.semh["dve"], v=vc: e.wait_ge(h, v))
    p.op("v", lambda e: e.tensor_scalar(BQ[:, :], BQ[:, :], SCALE, None,
                                        op0=ALU.mult))
    v = p.op("v", lambda e: e.tensor_scalar(DSOS[:, :], DS[:, :], VEC[:, 40:41],
                                            None, op0=ALU.mult), inc=True)
    p.rd_done("dve", v, PB[7])
    p.wr_done("dve", v, R_CONST)

    if DBG:
        p.use("y", R_CONST)
        p.op("y", lambda e: e.dma_start(dbg["vecI"], VEC[:, :]), inc=True)
        p.op("y", lambda e: e.dma_start(dbg["dsosI"], DSOS[:, :]), inc=True)

    def wslice_src(src, col0, width):
        return ap3(src.tensor, src.shape[1], col0, 128 * src.shape[1], 16, width)

    p.wr_acq("g", R_CTX)
    v = p.op("g", lambda e: e.dma_start(ap3(CTX, 32768, 0, 512, 16, 512),
                                        wslice_src(di["wvT"], 0, 512)), inc=True)
    wv_ready = {0: v}
    p.wr_done("gw", v, R_CTX)
    p.gbar()
    p.wr_acq("g", R_BV)
    v = p.op("g", lambda e: e.dma_start(BVROW[:, :], di["bvrow"][:, 0:512]), inc=True)
    bvrow_ready = {0: v}
    p.wr_done("gw", v, R_BV)
    p.gbar()
    p.wr_acq("g", R_C0[0], R_C0[1])
    p.op("g", lambda e: e.dma_start(ap3(WQK, 8192, 0, 128, 16, 128),
                                    wslice_src(di["wkT"], 0, 128)), inc=True)
    v = p.op("g", lambda e: e.dma_start(ap3(WQK, 8192, 2048, 128, 16, 128),
                                        wslice_src(di["wqT"], 0, 128)), inc=True)
    wqk_ready = {0: v}
    p.wr_done("gw", v, R_C0[0], R_C0[1])
    p.gbar()

    def SPC():
        v = p.op("v", lambda e: e.tensor_copy(VEC[:, 47:48], VEC[:, 40:41]),
                 inc=True)
        h = p.semh["dve"]
        p.ops["v"].append(lambda e, h=h, v=v: e.wait_ge(h, v))

    def emit_ln_chunk(dst, dst_rowlen, col_of, tr_par0):
        p.use("v", R_A)
        p.wr_acq("v", R_VECLN)
        p.op("v", lambda e: e.reduce_sum(VEC[:, 32:33], XTLN[:, :], axis=AX.X))
        SPC()
        p.use("s", R_A)
        p.wr_acq("s", R_C1[0])
        sq = p.op("s", lambda e: e.activation(XNT[:, :], XTLN[:, :], AF.Square,
                                              accum_out=VEC[:, 33:34]), inc=True)
        p.wait("v", "act", sq)
        p.op("v", lambda e: e.tensor_scalar(VEC[:, 34:35], VEC[:, 32:33], 1.0 / DM, None, op0=ALU.mult))
        p.op("v", lambda e: e.tensor_scalar(VEC[:, 35:36], VEC[:, 33:34], 1.0 / DM, EPS,
                                            op0=ALU.mult, op1=ALU.add))
        SPC()
        p.op("v", lambda e: e.tensor_tensor(VEC[:, 39:40], VEC[:, 34:35], VEC[:, 34:35], op=ALU.mult))
        SPC()
        vv = p.op("v", lambda e: e.tensor_tensor(VEC[:, 35:36], VEC[:, 35:36], VEC[:, 39:40],
                                                 op=ALU.subtract), inc=True)
        p.wait("s", "dve", vv)
        sv = p.op("s", lambda e: e.activation(VEC[:, 36:37], VEC[:, 35:36], AF.Sqrt), inc=True)
        p.wait("v", "act", sv)
        p.op("v", lambda e: e.reciprocal(VEC[:, 37:38], VEC[:, 36:37]))
        SPC()
        vv = p.op("v", lambda e: e.tensor_scalar(VEC[:, 38:39], VEC[:, 34:35], -1.0,
                                                 VEC[:, 37:38], op0=ALU.mult,
                                                 op1=ALU.mult), inc=True)
        p.wr_done("dve", vv, R_VECLN)
        p.wait("s", "dve", vv)
        sv = p.op("s", lambda e: e.activation(XNT[:, :], XTLN[:, :], AF.Identity,
                                              bias=VEC[:, 38:39], scale=VEC[:, 37:38]), inc=True)
        p.rd_done("act", sv, R_A)
        p.rd_done("act", sv, R_VECLN)
        p.wr_done("act", sv, R_C1[0])
        p.use("t", R_C1[0], R_CONST)
        last = None
        for w in range(4):
            bi = 4 + (tr_par0 + w) % 2
            p.wr_acq("t", PB[bi])
            for j in range(4):
                c = w * 4 + j
                iv = p.op("t", lambda e, c=c, j=j, bi=bi: e.matmul(
                    PB_AP[bi][:, j * 128:(j + 1) * 128],
                    XNT[:, c * 128:(c + 1) * 128], IDENT[:, :],
                    start=True, stop=True), inc=(j == 3))
            p.wr_done("pe", iv, PB[bi])
            p.rd_done("pe", iv, R_C1[0])
            p.use("v", PB[bi])
            c0 = w * 4
            dap = ap3(dst, dst_rowlen, col_of(c0), col_of(c0 + 1) - col_of(c0), 4, 128)
            sap = ap3(PB_H[bi], PB_RL[bi], PB_OF[bi], 128, 4, 128)
            last = p.op("v", lambda e, dap=dap, sap=sap: e.tensor_copy(dap, sap), inc=True)
            p.rd_done("dve", last, PB[bi])
        return last

    for sc in range(16):
        p.wr_acq("y", R_A)
        iv = p.op("y", lambda e, sc=sc: e.dma_start(
            XTLN[:, :], di["x"][sc * 128:(sc + 1) * 128, :]), inc=True)
        p.wr_done("io", iv, R_A)
        if sc == 0:
            p.wr_acq("v", R_XN)
        lv = emit_ln_chunk(XN, 32768, lambda c, sc=sc: c * 2048 + sc * 128, sc * 4)
    p.wr_done("dve", lv, R_XN)

    if DBG:
        p.use("y", R_XN)
        p.op("y", lambda e: e.dma_start(dbg["xn"], XN[:, :]), inc=True)
        v = p.op("y", lambda e: e.dma_start(dbg["vecA"], VEC[:, :]), inc=True)
        p.rd_done("io", v, R_VECLN)

    qrot = [0]
    exp_after = {}
    d_after = {}

    def emit_v4(g):
        p.use("t", R_BV, R_CONST)
        p.wr_acq("t", PB[6])
        v = p.op("t", lambda e: e.matmul(PB_AP[6], ONESB[:, :], BVROW[:, :],
                                         start=True, stop=True), inc=True)
        p.wr_done("pe", v, PB[6])
        p.rd_done("pe", v, R_BV)
        p.use("s", PB[6])
        p.wr_acq("s", R_BV)
        sv = p.op("s", lambda e: e.activation(BVBC[:, :], PB_AP[6], AF.Identity), inc=True)
        p.rd_done("act", sv, PB[6])
        p.wr_done("act", sv, R_BV)
        p.wait("t", "gw", wv_ready[g])
        p.use("t", R_XN, R_CTX)
        for sc in range(16):
            p.wr_acq("t", PB[7])
            for c in range(NDC):
                v = p.op("t", lambda e, c=c, sc=sc, g=g: e.matmul(
                    PB_AP[7],
                    XN[:, c * 2048 + sc * 128: c * 2048 + (sc + 1) * 128],
                    CTX[:, g * 8192 + c * 512: g * 8192 + (c + 1) * 512],
                    start=(c == 0), stop=(c == NDC - 1)), inc=(c == NDC - 1))
            p.wr_done("pe", v, PB[7])
            p.rd_done("pe", v, R_CTX, R_XN)
            p.use("s", PB[7])
            if sc == 0:
                p.wr_acq("s", R_A)
            sv = p.op("s", lambda e, sc=sc: e.activation(
                V4[:, sc * 512:(sc + 1) * 512], PB_AP[7], AF.Identity), inc=True)
            p.rd_done("act", sv, PB[7])
            p.use("v", R_BV)
            vv = p.op("v", lambda e, sc=sc: e.tensor_tensor(
                V4[:, sc * 512:(sc + 1) * 512], V4[:, sc * 512:(sc + 1) * 512],
                BVBC[:, :], op=ALU.add), inc=True)
            p.rd_done("dve", vv, R_BV)
        p.wr_done("dve", vv, R_A)

    def emit_kq(h):
        par = h % 2
        p.wait("t", "gw", wqk_ready[h])
        p.use("t", R_XN)
        wqk_res = (R_C0[0], R_C0[1]) if par == 0 else (R_C1[0], R_C1[1])
        for qk in range(2):
            for sg in range(4):
                p.wr_acq("t", PB[7])
                for c in range(NDC):
                    wcol = par * 4096 + qk * 2048 + c * 128
                    v = p.op("t", lambda e, c=c, sg=sg, wcol=wcol: e.matmul(
                        PB_AP[7], WQK[:, wcol:wcol + 128],
                        XN[:, c * 2048 + sg * 512: c * 2048 + (sg + 1) * 512],
                        start=(c == 0), stop=(c == NDC - 1)), inc=(c == NDC - 1))
                p.wr_done("pe", v, PB[7])
                p.rd_done("pe", v, R_XN, *wqk_res)
                dcol = par * 4096 + qk * 2048 + sg * 512
                if qk == 0:
                    p.use("s", PB[7], R_CONST)
                    if sg == 0:
                        p.wr_acq("s", R_B[par])
                    sv = p.op("s", lambda e, dcol=dcol, h=h: e.activation(
                        KQ[:, dcol:dcol + 512], PB_AP[7], AF.Identity,
                        bias=BK[:, h:h + 1]), inc=True)
                    p.rd_done("act", sv, PB[7])
                    last_eng, last_v = "act", sv
                else:
                    p.use("v", PB[7], R_CONST)
                    p.wait("v", "act", p.cnt("act"))
                    sv = p.op("v", lambda e, dcol=dcol, h=h: e.tensor_scalar(
                        KQ[:, dcol:dcol + 512], PB_AP[7], SCALE, BQ[:, h:h + 1],
                        op0=ALU.mult, op1=ALU.add), inc=True)
                    p.rd_done("dve", sv, PB[7])
                    last_eng, last_v = "dve", sv
        p.wr_done(last_eng, last_v, R_B[par])

    def rd_wqk(par, v):
        if par == 0:
            p.rd_done("pe", v, R_C0[0], R_C0[1])
        else:
            p.rd_done("pe", v, R_C1[0], R_C1[1])

    def emit_scores(t, h):
        par = t % 2
        vb = par * 16
        qg = t // 2
        chunks = 2 * qg + 2
        nkt = (chunks + 3) // 4
        off = (t % 4) * 128
        kcol = (h % 2) * 4096
        qcol = kcol + 2048
        p.use("t", R_B[h % 2])
        for kt in range(nkt):
            bi = qrot[0] % 4
            qrot[0] += 1
            p.wr_acq("t", PB[bi])
            v = p.op("t", lambda e, t=t, kt=kt, bi=bi: e.matmul(
                PB_AP[bi], KQ[:, qcol + t * 128: qcol + (t + 1) * 128],
                KQ[:, kcol + kt * 512: kcol + (kt + 1) * 512],
                start=True, stop=True), inc=True)
            p.wr_done("pe", v, PB[bi])
            p.rd_done("pe", v, R_B[h % 2])
            p.use("v", PB[bi])
            if kt == nkt - 1:
                p.use("v", R_MK)
                vv = p.op("v", lambda e, bi=bi, off=off: e.tensor_tensor(
                    PB_AP[bi][:, off:512], PB_AP[bi][:, off:512],
                    MASK[:, 0:512 - off], op=ALU.add), inc=True)
                p.rd_done("dve", vv, R_MK)
            if kt == 0:
                p.wr_acq("v", R_DTI[par])
            vv = p.op("v", lambda e, bi=bi, c=vb + kt: e.reduce_max(
                VEC[:, c:c + 1], PB_AP[bi], axis=AX.X, negate=True), inc=True)
            p.wait("s", "dve", vv)
            p.use("s", PB[bi])
            if kt == 0:
                p.wr_acq("s", R_PR[par])
            sv = p.op("s", lambda e, bi=bi, par=par, kt=kt, c=vb + kt, a=vb + 4 + kt: e.activation(
                PROBS[:, par * 2048 + kt * 512: par * 2048 + (kt + 1) * 512],
                PB_AP[bi], AF.Exp, bias=VEC[:, c:c + 1],
                accum_out=VEC[:, a:a + 1]), inc=True)
            p.rd_done("act", sv, PB[bi])
        p.wr_done("act", sv, R_PR[par])
        exp_after[t] = sv
        p.wait("v", "act", sv)
        if nkt == 1:
            p.op("v", lambda e, vb=vb: e.reciprocal(VEC[:, vb + 14:vb + 15],
                                                    VEC[:, vb + 4:vb + 5]))
            p.use("v", R_CONST)
            SPC()
            dv = p.op("v", lambda e, par=par, vb=vb: e.tensor_scalar(
                DTI[:, par * 512: par * 512 + 128], IDENT[:, :],
                VEC[:, vb + 14:vb + 15], None, op0=ALU.mult), inc=True)
        else:
            p.op("v", lambda e, vb=vb, nkt=nkt: e.tensor_reduce(
                VEC[:, vb + 12:vb + 13], VEC[:, vb:vb + nkt], axis=AX.X, op=ALU.min))
            SPC()
            vv = p.op("v", lambda e, vb=vb, nkt=nkt: e.tensor_scalar(
                VEC[:, vb + 8:vb + 8 + nkt], VEC[:, vb:vb + nkt],
                VEC[:, vb + 12:vb + 13], -1.0, op0=ALU.subtract,
                op1=ALU.mult), inc=True)
            p.wait("s", "dve", vv)
            sv2 = p.op("s", lambda e, vb=vb, nkt=nkt: e.activation(
                VEC[:, vb + 8:vb + 8 + nkt], VEC[:, vb + 8:vb + 8 + nkt],
                AF.Exp), inc=True)
            p.wait("v", "act", sv2)
            p.op("v", lambda e, vb=vb, nkt=nkt: e.tensor_tensor(
                VEC[:, vb + 4:vb + 4 + nkt],
                VEC[:, vb + 4:vb + 4 + nkt], VEC[:, vb + 8:vb + 8 + nkt], op=ALU.mult))
            SPC()
            p.op("v", lambda e, vb=vb, nkt=nkt: e.reduce_sum(
                VEC[:, vb + 13:vb + 14], VEC[:, vb + 4:vb + 4 + nkt], axis=AX.X))
            SPC()
            p.op("v", lambda e, vb=vb: e.reciprocal(VEC[:, vb + 14:vb + 15],
                                                    VEC[:, vb + 13:vb + 14]))
            SPC()
            p.op("v", lambda e, vb=vb, nkt=nkt: e.tensor_scalar(
                VEC[:, vb + 8:vb + 8 + nkt], VEC[:, vb + 8:vb + 8 + nkt],
                VEC[:, vb + 14:vb + 15], None, op0=ALU.mult))
            p.use("v", R_CONST)
            SPC()
            for kt in range(nkt):
                dv = p.op("v", lambda e, par=par, kt=kt, vb=vb: e.tensor_scalar(
                    DTI[:, par * 512 + kt * 128: par * 512 + (kt + 1) * 128],
                    IDENT[:, :], VEC[:, vb + 8 + kt:vb + 9 + kt], None,
                    op0=ALU.mult), inc=True)
        p.wr_done("dve", dv, R_DTI[par])
        d_after[t] = dv

    def emit_transposes(t):
        par = t % 2
        qg = t // 2
        chunks = 2 * qg + 2
        p.wait("t", "dve", d_after[t])
        p.use("t", R_PR[par], R_DTI[par])
        for b0 in range(0, chunks, 4):
            nb = min(4, chunks - b0)
            bi = 4 + (b0 // 4) % 2
            p.wr_acq("t", PB[bi])
            for i in range(nb):
                kc = b0 + i
                v = p.op("t", lambda e, kc=kc, i=i, bi=bi, par=par: e.matmul(
                    PB_AP[bi][:, i * 128:(i + 1) * 128],
                    PROBS[:, par * 2048 + kc * 128: par * 2048 + (kc + 1) * 128],
                    DTI[:, par * 512 + (kc // 4) * 128: par * 512 + (kc // 4 + 1) * 128],
                    start=True, stop=True), inc=(i == nb - 1))
            p.wr_done("pe", v, PB[bi])
            p.rd_done("pe", v, R_PR[par], R_DTI[par])
            p.use("v", PB[bi])
            if b0 == 0:
                p.wr_acq("v", R_PT)
            dap = ap3(PROBST, 4096, b0 * 256 + par * 128, 256, nb, 128)
            sap = ap3(PB_H[bi], PB_RL[bi], PB_OF[bi], 128, nb, 128)
            vv = p.op("v", lambda e, dap=dap, sap=sap: e.tensor_copy(dap, sap), inc=True)
            p.rd_done("dve", vv, PB[bi])
        p.wr_done("dve", vv, R_PT)

    def emit_pv(qg, h):
        j = h % 4
        chunks = 2 * qg + 2
        p.use("t", R_PT, R_A)
        p.wr_acq("t", PB[6])
        for kc in range(chunks):
            v = p.op("t", lambda e, kc=kc, j=j: e.matmul(
                PB_AP[6][:, 0:256],
                V4[:, kc * 512 + j * 128: kc * 512 + (j + 1) * 128],
                PROBST[:, kc * 256:(kc + 1) * 256],
                start=(kc == 0), stop=(kc == chunks - 1)), inc=(kc == chunks - 1))
        p.wr_done("pe", v, PB[6])
        p.rd_done("pe", v, R_PT, R_A)
        p.use("v", PB[6])
        vv = p.op("v", lambda e, qg=qg, h=h: e.tensor_copy(
            CTX[:, h * 2048 + qg * 256: h * 2048 + (qg + 1) * 256],
            PB_AP[6][:, 0:256]), inc=True)
        p.rd_done("dve", vv, PB[6])
        p.wr_done("dve", vv, R_CTX)

    for g in range(4):
        emit_v4(g)
        if g < 3:
            v = p.op("g", lambda e, g=g: e.dma_start(
                ap3(CTX, 32768, (g + 1) * 8192, 512, 16, 512),
                wslice_src(di["wvT"], (g + 1) * 512, 512)), inc=True)
            wv_ready[g + 1] = v
            p.wr_acq("g", R_BV)
            v = p.op("g", lambda e, g=g: e.dma_start(
                BVROW[:, :], di["bvrow"][:, (g + 1) * 512:(g + 2) * 512]), inc=True)
            bvrow_ready[g + 1] = v
            p.wr_done("gw", v, R_BV)
            p.gbar()
        for j4 in range(4):
            h = 4 * g + j4
            emit_kq(h)
            if h < NH - 1:
                parn = (h + 1) % 2
                rs = (R_C0[0], R_C0[1]) if parn == 0 else (R_C1[0], R_C1[1])
                p.wr_acq("g", *rs)
                p.op("g", lambda e, h=h, parn=parn: e.dma_start(
                    ap3(WQK, 8192, parn * 4096, 128, 16, 128),
                    wslice_src(di["wkT"], (h + 1) * 128, 128)), inc=True)
                v = p.op("g", lambda e, h=h, parn=parn: e.dma_start(
                    ap3(WQK, 8192, parn * 4096 + 2048, 128, 16, 128),
                    wslice_src(di["wqT"], (h + 1) * 128, 128)), inc=True)
                wqk_ready[h + 1] = v
                p.wr_done("gw", v, *rs)
                p.gbar()
            for t in range(16):
                emit_scores(t, h)
                if t >= 1:
                    emit_transposes(t - 1)
                if t >= 2 and t % 2 == 0:
                    emit_pv((t - 2) // 2, h)
            emit_transposes(15)
            emit_pv(7, h)

    if DBG:
        p.use("y", R_CTX, R_A, R_B[0], R_B[1], R_PT)
        p.use("y", R_PR[0], R_PR[1], R_DTI[0], R_DTI[1])
        p.op("y", lambda e: e.dma_start(dbg["ctx"], CTX[:, :]), inc=True)
        p.op("y", lambda e: e.dma_start(dbg["kqv"][:, 0:8192], V4[:, :]), inc=True)
        p.op("y", lambda e: e.dma_start(dbg["kqv"][:, 8192:16384], KQ[:, :]), inc=True)
        p.op("y", lambda e: e.dma_start(dbg["pt"], PROBST[:, :]), inc=True)
        p.op("y", lambda e: e.dma_start(dbg["vecB"], VEC[:, :]), inc=True)
        p.op("y", lambda e: e.dma_start(dbg["dti"], DTI[:, :]), inc=True)
        v = p.op("y", lambda e: e.dma_start(dbg["probs"], PROBS[:, :]), inc=True)
        p.rd_done("io", v, R_CTX, R_A, R_B[0], R_B[1], R_PT)
        p.rd_done("io", v, R_PR[0], R_PR[1], R_DTI[0], R_DTI[1], R_VECLN)

    for ps in range(NPASS):
        s0 = ps * SP
        first = ps == 0
        for sc4 in range(4):
            p.wr_acq("y", R_A)
            iv = p.op("y", lambda e, s0=s0, sc4=sc4: e.dma_start(
                XTLN[:, :], di["x"][s0 + sc4 * 128: s0 + (sc4 + 1) * 128, :]), inc=True)
            p.wr_done("io", iv, R_A)
            if sc4 == 0:
                p.wr_acq("v", R_PR[0], R_PR[1], R_PT)
            lv = emit_ln_chunk(XNS, 8192, lambda c, sc4=sc4: c * 512 + sc4 * 128, sc4 * 4)
        p.wr_done("dve", lv, R_PR[0], R_PR[1], R_PT)
        R_XNS = (R_PR[0], R_PR[1], R_PT)
        if DBG and ps == 0:
            p.use("y", *R_XNS)
            v = p.op("y", lambda e: e.dma_start(dbg["hid"][:, 0:8192], XNS[:, :]), inc=True)
            p.rd_done("io", v, *R_XNS)

        for hc in range(NHC):
            par = hc % 2
            ucol = par * 2048
            if first:
                p.wr_acq("g", R_C0[par])
                v = p.op("g", lambda e, ucol=ucol, hc=hc: e.dma_start(
                    ap3(UPW, 4096, ucol, 128, 16, 128),
                    wslice_src(di["upT"], hc * 128, 128)), inc=True)
                p.wr_done("gw", v, R_C0[par])
                p.gbar()
                p.use("s", R_C0[par])
                sv = p.op("s", lambda e, ucol=ucol: e.sign(
                    UPW[:, ucol:ucol + 2048], UPW[:, ucol:ucol + 2048]), inc=True)
                p.wr_done("act", sv, R_C0[par])
                p.use("y", R_C0[par])
                yv = p.op("y", lambda e, ucol=ucol, hc=hc: e.dma_start(
                    up_s[hc * 128:(hc + 1) * 128, :], UPW[:, ucol:ucol + 2048]), inc=True)
                p.rd_done("io", yv, R_C0[par])
            else:
                p.wr_acq("y", R_C0[par])
                v = p.op("y", lambda e, ucol=ucol, hc=hc: e.dma_start(
                    UPW[:, ucol:ucol + 2048], up_s[hc * 128:(hc + 1) * 128, :]), inc=True)
                p.wr_done("io", v, R_C0[par])
            bi = 6 + (hc % 2)
            p.use("t", R_C0[par], *R_XNS)
            p.wr_acq("t", PB[bi])
            for c in range(NDC):
                v = p.op("t", lambda e, c=c, ucol=ucol, bi=bi: e.matmul(
                    PB_AP[bi], UPW[:, ucol + c * 128: ucol + (c + 1) * 128],
                    XNS[:, c * 512:(c + 1) * 512],
                    start=(c == 0), stop=(c == NDC - 1)), inc=(c == NDC - 1))
            p.wr_done("pe", v, PB[bi])
            p.rd_done("pe", v, R_C0[par], *R_XNS)
            p.use("s", PB[bi], R_CONST)
            if hc == 0:
                p.wr_acq("s", R_XN)
            sv = p.op("s", lambda e, hc=hc, bi=bi: e.activation(
                XN[:, hc * 512:(hc + 1) * 512], PB_AP[bi], AF.Relu,
                scale=US[:, hc:hc + 1]), inc=True)
            p.rd_done("act", sv, PB[bi])
        p.wr_done("act", sv, R_XN)
        if DBG and ps == 0:
            p.use("y", R_XN)
            v = p.op("y", lambda e: e.dma_start(dbg["hid"][:, 8192:16384], XN[:, 0:8192]), inc=True)
            p.rd_done("io", v, R_XN)

        for ec in range(NEC):
            par = ec % 2
            dcol = par * 8192
            pcol = par * 2048
            pair = ec % 4
            bp_, bf_ = 2 * pair, 2 * pair + 1
            dn_res = (R_A,) if par == 0 else (R_B[0], R_B[1])
            if first:
                p.wr_acq("g", *dn_res)
                v = p.op("g", lambda e, dcol=dcol, ec=ec: e.dma_start(
                    ap3(DNW, 16384, dcol, 128, 64, 128),
                    ap3(di["dnT"].tensor, DM, ec * 128, 128 * DM, 64, 128)), inc=True)
                p.wr_done("gw", v, *dn_res)
                p.gbar()
                p.use("s", *dn_res)
                sv = p.op("s", lambda e, dcol=dcol: e.sign(
                    DNW[:, dcol:dcol + 8192], DNW[:, dcol:dcol + 8192]), inc=True)
                p.wr_done("act", sv, *dn_res)
                p.use("y", *dn_res)
                yv = p.op("y", lambda e, dcol=dcol, ec=ec: e.dma_start(
                    dn_s[ec * 128:(ec + 1) * 128, :], DNW[:, dcol:dcol + 8192]), inc=True)
                p.rd_done("io", yv, *dn_res)
                p.wr_acq("g", R_C1[par])
                v = p.op("g", lambda e, pcol=pcol, ec=ec: e.dma_start(
                    ap3(PJW, 4096, pcol, 128, 16, 128),
                    wslice_src(di["wpT"], ec * 128, 128)), inc=True)
                p.wr_done("gw", v, R_C1[par])
                p.gbar()
                p.use("y", R_C1[par])
                yv = p.op("y", lambda e, pcol=pcol, ec=ec: e.dma_start(
                    pj_s[ec * 128:(ec + 1) * 128, :], PJW[:, pcol:pcol + 2048]), inc=True)
                p.rd_done("io", yv, R_C1[par])
            else:
                p.wr_acq("y", *dn_res)
                v = p.op("y", lambda e, dcol=dcol, ec=ec: e.dma_start(
                    DNW[:, dcol:dcol + 8192], dn_s[ec * 128:(ec + 1) * 128, :]), inc=True)
                p.wr_done("io", v, *dn_res)
                p.wr_acq("y", R_C1[par])
                v = p.op("y", lambda e, pcol=pcol, ec=ec: e.dma_start(
                    PJW[:, pcol:pcol + 2048], pj_s[ec * 128:(ec + 1) * 128, :]), inc=True)
                p.wr_done("io", v, R_C1[par])
            xres_t = XRES0 if par == 0 else XRES1
            xres_r = R_MK if par == 0 else R_BV
            p.wr_acq("y", xres_r)
            xv = p.op("y", lambda e, ec=ec, s0=s0, xres_t=xres_t: e.dma_start(
                xres_t[:, :], di["xT"][ec * 128:(ec + 1) * 128, s0:s0 + 512]), inc=True)
            p.wr_done("io", xv, xres_r)
            p.use("t", R_C1[par], R_CTX)
            p.wr_acq("t", PB[bp_])
            for c in range(NDC):
                v = p.op("t", lambda e, c=c, pcol=pcol, bp_=bp_, s0=s0: e.matmul(
                    PB_AP[bp_], PJW[:, pcol + c * 128: pcol + (c + 1) * 128],
                    CTX[:, c * 2048 + s0: c * 2048 + s0 + 512],
                    start=(c == 0), stop=(c == NDC - 1)), inc=(c == NDC - 1))
            p.wr_done("pe", v, PB[bp_])
            p.rd_done("pe", v, R_C1[par], R_CTX)
            p.use("t", *dn_res)
            p.use("t", R_XN)
            p.wr_acq("t", PB[bf_])
            for hc in range(NHC):
                v = p.op("t", lambda e, hc=hc, dcol=dcol, bf_=bf_: e.matmul(
                    PB_AP[bf_], DNW[:, dcol + hc * 128: dcol + (hc + 1) * 128],
                    XN[:, hc * 512:(hc + 1) * 512],
                    start=(hc == 0), stop=(hc == NHC - 1)), inc=(hc == NHC - 1))
            p.wr_done("pe", v, PB[bf_])
            p.rd_done("pe", v, R_XN)
            p.rd_done("pe", v, *dn_res)
            tp = ec % 2
            t1 = T1_0 if tp == 0 else T1_1
            p.use("s", PB[bf_], R_CONST)
            p.wr_acq("s", R_T1[tp])
            sv = p.op("s", lambda e, bf_=bf_, ec=ec, t1=t1: e.activation(
                t1[:, :], PB_AP[bf_], AF.Identity, scale=DSOS[:, ec:ec + 1],
                bias=BP[:, ec:ec + 1]), inc=True)
            p.rd_done("act", sv, PB[bf_])
            p.wr_done("act", sv, R_T1[tp])
            p.use("v", PB[bp_], R_T1[tp], R_CONST)
            p.op("v", lambda e, bp_=bp_, t1=t1: e.tensor_tensor(
                t1[:, :], PB_AP[bp_], t1[:, :], op=ALU.add))
            p.use("v", xres_r)
            SPC()
            vv = p.op("v", lambda e, t1=t1, xres_t=xres_t: e.tensor_tensor(
                t1[:, :], t1[:, :], xres_t[:, :], op=ALU.add), inc=True)
            p.rd_done("dve", vv, PB[bp_], xres_r)
            p.wr_done("dve", vv, R_T1[tp])
            p.use("y", R_T1[tp])
            ov = p.op("y", lambda e, ec=ec, s0=s0, t1=t1: e.dma_start(
                y[ec * 128:(ec + 1) * 128, s0:s0 + 512], t1[:, :]), inc=True)
            p.rd_done("io", ov, R_T1[tp])

    if DBG:
        p.use("y", R_CONST)
        p.op("y", lambda e: e.dma_start(dbg["misc"][:, 0:16], DSOS[:, :]), inc=True)
        p.op("y", lambda e: e.dma_start(dbg["misc"][:, 16:32], DS[:, :]), inc=True)
        p.op("y", lambda e: e.dma_start(dbg["misc"][:, 32:48], BP[:, :]), inc=True)
        p.op("y", lambda e: e.dma_start(dbg["misc"][:, 48:64], BQ[:, :]), inc=True)
        p.op("y", lambda e: e.dma_start(dbg["misc"][:, 64:80], US[:, 0:16]), inc=True)
        p.use("y", R_A, R_B[0], R_B[1])
        p.op("y", lambda e: e.dma_start(dbg["dnw"], DNW[:, 0:8192]), inc=True)
        p.use("y", R_XN)
        p.op("y", lambda e: e.dma_start(dbg["hid2"], XN[:, :]), inc=True)
        p.use("y", R_T1[0], R_T1[1])
        p.op("y", lambda e: e.dma_start(dbg["t1"][:, 0:512], T1_0[:, :]), inc=True)
        p.op("y", lambda e: e.dma_start(dbg["t1"][:, 512:1024], T1_1[:, :]), inc=True)

    p.wait("y", "io", p.cnt("io"))
    p.wait("g", "gw", p.cnt("gw"))

    with nc.Block() as block:
        @block.tensor
        def _(t):
            for f in p.ops["t"]:
                f(t)

        @block.vector
        def _(v):
            for f in p.ops["v"]:
                f(v)

        @block.scalar
        def _(s):
            for f in p.ops["s"]:
                f(s)

        @block.gpsimd
        def _(g):
            for f in p.ops["g"]:
                f(g)

        @block.sync
        def _(sy):
            for f in p.ops["y"]:
                f(sy)
    stack.close()
    return nc


def kernel(L0_attn_x, L0_attn_w_attn, L0_attn_b_attn, L0_attn_w_proj,
           L0_attn_b_proj, L0_ffn_up_weight, L0_ffn_down_weight,
           L0_ffn_up_scale, L0_ffn_down_scale, L0_ffn_output_scale):
    x = np.ascontiguousarray(np.asarray(L0_attn_x, dtype=np.float32))
    w_attn = np.asarray(L0_attn_w_attn, dtype=np.float32)
    b_attn = np.asarray(L0_attn_b_attn, dtype=np.float32)
    w_proj = np.asarray(L0_attn_w_proj, dtype=np.float32)
    b_proj = np.asarray(L0_attn_b_proj, dtype=np.float32)
    up_w = np.asarray(L0_ffn_up_weight, dtype=np.float32)
    dn_w = np.asarray(L0_ffn_down_weight, dtype=np.float32)
    us = np.asarray(L0_ffn_up_scale, dtype=np.float32)
    ds = np.asarray(L0_ffn_down_scale, dtype=np.float32)
    osc = np.asarray(L0_ffn_output_scale, dtype=np.float32)

    wsp = w_attn.reshape(NH, 3, HD, DM)
    wqT = np.ascontiguousarray(wsp[:, 0].reshape(DM, DM).T)
    wkT = np.ascontiguousarray(wsp[:, 1].reshape(DM, DM).T)
    wvT = np.ascontiguousarray(wsp[:, 2].reshape(DM, DM).T)
    bsp = b_attn.reshape(NH, 3, HD)
    bq2d = np.ascontiguousarray(bsp[:, 0].reshape(NH, HD).T)
    bk2d = np.ascontiguousarray(bsp[:, 1].reshape(NH, HD).T)
    bvrow = np.ascontiguousarray(bsp[:, 2].reshape(1, DM))
    wpT = np.ascontiguousarray(w_proj.T)
    bp2d = np.ascontiguousarray(b_proj.reshape(NEC, 128).T)
    upT = np.ascontiguousarray(up_w.T)
    dnT = np.ascontiguousarray(dn_w.T)
    us2d = np.ascontiguousarray(us.reshape(NHC, 128).T)
    ds2d = np.ascontiguousarray(ds.reshape(NEC, 128).T)
    osc2d = np.ascontiguousarray(osc.reshape(1, 1))

    mask = np.zeros((128, 512), dtype=np.float32)
    for i in range(128):
        mask[i, i + 1:] = NEG
    ident = np.eye(128, dtype=ml_dtypes.bfloat16)
    onesb = np.ones((1, 128), dtype=ml_dtypes.bfloat16)
    onesf = np.ones((1, 128), dtype=np.float32)

    shared = dict(wqT=wqT, wkT=wkT, wvT=wvT, bq2d=bq2d, bk2d=bk2d, bvrow=bvrow,
                  wpT=wpT, bp2d=bp2d, upT=upT, dnT=dnT, us2d=us2d, ds2d=ds2d,
                  osc=osc2d, maskc=mask, identc=ident, onesb=onesb, onesf=onesf)
    in_maps = []
    for b in range(8):
        m = dict(shared)
        m["x"] = np.ascontiguousarray(x[b])
        m["xT"] = np.ascontiguousarray(x[b].T)
        in_maps.append(m)

    nc = build_nc()
    res = run_bass_kernel_spmd(nc, in_maps, core_ids=list(range(8)))
    if DBG:
        kernel._dbg = [dict(r) for r in (res.results[i] for i in range(8))]
    out = np.stack([np.asarray(r["y"]).T for r in res.results], axis=0)
    return np.ascontiguousarray(out.astype(np.float32))


# version 11
# speedup vs baseline: 1.2268x; 1.2268x over previous
import math
import os

import numpy as np
import ml_dtypes
import concourse.bass as bass
import concourse.mybir as mybir
from concourse.bass_utils import run_bass_kernel_spmd

F32 = mybir.dt.float32
BF16 = mybir.dt.bfloat16
AF = mybir.ActivationFunctionType
ALU = mybir.AluOpType
AX = mybir.AxisListType

S = 2048
DM = 2048
NH = 16
HD = 128
DH = 8192
NDC = DM // 128
NHC = DH // 128
NEC = DM // 128
SCALE = 1.0 / math.sqrt(HD)
NEG = -1.0e30
EPS = 1.0e-5
SP = 512
NPASS = S // SP

DBG = bool(os.environ.get("KDBG"))

ENG_SEM = {"t": "pe", "v": "dve", "s": "act", "g": "gw", "y": "io"}


class R:
    __slots__ = ("wr", "rd")

    def __init__(self):
        self.wr = None
        self.rd = {}


class P:
    def __init__(self, nc):
        self.nc = nc
        self.ops = {"t": [], "v": [], "s": [], "g": [], "y": []}
        self.semh = {}
        self.cnts = {}
        self.hw = {}

    def sem(self, name, handle):
        self.semh[name] = handle
        self.cnts[name] = 0

    def cnt(self, n):
        return self.cnts[n]

    def wait(self, eng, sem, val):
        if val <= 0 or ENG_SEM[eng] == sem:
            return
        k = (eng, sem)
        if self.hw.get(k, 0) >= val:
            return
        self.hw[k] = val
        h = self.semh[sem]
        self.ops[eng].append(lambda e, h=h, v=val: e.wait_ge(h, v))

    def op(self, eng, fn, inc=False):
        if not inc:
            self.ops[eng].append(fn)
            return None
        sem = ENG_SEM[eng]
        step = 16 if sem in ("gw", "io") else 1
        self.cnts[sem] += step
        v = self.cnts[sem]
        h = self.semh[sem]
        self.ops[eng].append(lambda e, fn=fn, h=h, s=step: fn(e).then_inc(h, s))
        return v

    def gbar(self):
        v = self.cnts["gw"]
        h = self.semh["gw"]
        self.ops["g"].append(lambda e, h=h, v=v: e.wait_ge(h, v))

    def use(self, eng, *rs):
        for r in rs:
            if r.wr:
                self.wait(eng, r.wr[0], r.wr[1])

    def rd_done(self, sem, cnt, *rs):
        for r in rs:
            if r.rd.get(sem, 0) < cnt:
                r.rd[sem] = cnt

    def wr_acq(self, eng, *rs):
        for r in rs:
            if r.wr:
                self.wait(eng, r.wr[0], r.wr[1])
            for s, c in r.rd.items():
                self.wait(eng, s, c)
            r.rd = {}

    def wr_done(self, sem, cnt, *rs):
        for r in rs:
            r.wr = (sem, cnt)


def ap3(handle, rowlen, offset, step, n, width):
    return bass.AP(handle, offset, [[rowlen, 128], [step, n], [1, width]])


BASE = 16512
OFF_XN = BASE + 0
OFF_CTX = BASE + 65536
OFF_V4 = BASE + 131072
OFF_KQ = BASE + 147456
OFF_WQK = BASE + 163840
OFF_PROBS = BASE + 180224
OFF_PROBST = BASE + 188416
OFF_MASK = BASE + 196608
OFF_BVBC = BASE + 198656
OFF_T1 = BASE + 200704
OFF_IDENT = BASE + 204800
OFF_ONESB = BASE + 205056
OFF_ONESF = BASE + 205312
OFF_DTI = BASE + 205824
OFF_VEC = BASE + 207872
OFF_VECL = BASE + 208704
OFF_STATS = BASE + 208832
OFF_BQ = BASE + 208064
OFF_BK = BASE + 208128
OFF_BP = BASE + 208192
OFF_DS = BASE + 208256
OFF_DSOS = BASE + 208320
OFF_OSC = BASE + 208384
OFF_US = BASE + 208448
SB_CAP = 229344
assert OFF_STATS + 128 <= SB_CAP



def build_nc():
    nc = bass.Bass("TRN2", target_bir_lowering=False)

    di = {}
    def inp(name, shape, dt=F32):
        di[name] = nc.dram_tensor(name, shape, dt, kind="ExternalInput").ap()
    inp("x", [S, DM]); inp("xT", [DM, S])
    inp("wqT", [DM, DM]); inp("wkT", [DM, DM]); inp("wvT", [512, DH])
    inp("bq2d", [128, NH]); inp("bk2d", [128, NH]); inp("bvrow", [1, DM])
    inp("wpT", [DM, DM]); inp("bp2d", [128, NEC])
    inp("upT", [DH, DM]); inp("dnT", [DM, DH])
    inp("us2d", [128, NHC]); inp("ds2d", [128, NEC]); inp("osc", [1, 1])
    inp("maskc", [128, 512]); inp("identc", [128, 128], BF16)
    inp("onesb", [1, 128], BF16); inp("onesf", [1, 128])
    y = nc.dram_tensor("y", [DM, S], F32, kind="ExternalOutput").ap()
    up_s = nc.dram_tensor("up_s", [DH, DM], BF16, kind="Internal").ap()
    dn_s = nc.dram_tensor("dn_s", [DM, DH], BF16, kind="Internal").ap()
    pj_s = nc.dram_tensor("pj_s", [DM, DM], BF16, kind="Internal").ap()
    dbg = {}
    if DBG:
        for nm, w in [("xn", 32768), ("ctx", 32768), ("hid", 16384),
                      ("kqv", 16384), ("pt", 4096)]:
            dbg[nm] = nc.dram_tensor("dbg_" + nm, [128, w], BF16,
                                     kind="ExternalOutput").ap()
        for nm, w in [("misc", 80), ("t1", 1024), ("vecI", 48), ("vecA", 48),
                      ("vecB", 48), ("dsosI", 16)]:
            dbg[nm] = nc.dram_tensor("dbg_" + nm, [128, w], F32,
                                     kind="ExternalOutput").ap()
        for nm, w in [("dti", 1024), ("probs", 4096)]:
            dbg[nm] = nc.dram_tensor("dbg_" + nm, [128, w], BF16,
                                     kind="ExternalOutput").ap()
        dbg["dnw"] = nc.dram_tensor("dbg_dnw", [128, 8192], BF16,
                                    kind="ExternalOutput").ap()
        dbg["hid2"] = nc.dram_tensor("dbg_hid2", [128, 32768], BF16,
                                     kind="ExternalOutput").ap()

    A = lambda name, shape, dt, off: nc.alloc_sbuf_tensor_at(name, shape, dt, offset=off)
    XN = A("XN", [128, 32768], BF16, OFF_XN)
    CTX = A("CTX", [128, 32768], BF16, OFF_CTX)
    V4 = A("V4", [128, 8192], BF16, OFF_V4)
    KQ = A("KQ", [128, 8192], BF16, OFF_KQ)
    WQK = A("WQK", [128, 8192], BF16, OFF_WQK)
    DNW = A("DNW", [128, 16384], BF16, OFF_V4)
    UPW = A("UPW", [128, 4096], BF16, OFF_WQK)
    PJW = A("PJW", [128, 4096], BF16, OFF_WQK + 8192)
    XNT0 = A("XNT0", [128, 2048], BF16, OFF_WQK + 8192)
    XNT1 = A("XNT1", [128, 2048], BF16, OFF_WQK + 8192 + 4096)
    PROBS = A("PROBS", [128, 4096], BF16, OFF_PROBS)
    PROBST = A("PROBST", [128, 4096], BF16, OFF_PROBST)
    XNS = A("XNS", [128, 8192], BF16, OFF_PROBS)
    MASK = A("MASK", [128, 512], F32, OFF_MASK)
    BVBC = A("BVBC", [128, 512], BF16, OFF_BVBC)
    BVROW = A("BVROW", [1, 512], BF16, OFF_BVBC + 1024)
    XRES0 = A("XRES0", [128, 512], F32, OFF_MASK)
    XRES1 = A("XRES1", [128, 512], F32, OFF_BVBC)
    T1_0 = A("T1_0", [128, 512], F32, OFF_T1)
    T1_1 = A("T1_1", [128, 512], F32, OFF_T1 + 2048)
    XTLN0 = A("XTLN0", [128, 2048], F32, OFF_V4)
    XTLN1 = A("XTLN1", [128, 2048], F32, OFF_V4 + 8192)
    IDENT = A("IDENT", [128, 128], BF16, OFF_IDENT)
    ONESB = A("ONESB", [1, 128], BF16, OFF_ONESB)
    ONESF = A("ONESF", [1, 128], F32, OFF_ONESF)
    DTI = A("DTI", [128, 1024], BF16, OFF_DTI)
    VEC = A("VEC", [128, 48], F32, OFF_VEC)
    VECL = A("VECL", [128, 32], F32, OFF_VECL)
    STATS = A("STATS", [128, 32], F32, OFF_STATS)
    BQ = A("BQ", [128, NH], F32, OFF_BQ)
    BK = A("BK", [128, NH], F32, OFF_BK)
    BP = A("BP", [128, NEC], F32, OFF_BP)
    DS = A("DS", [128, NEC], F32, OFF_DS)
    DSOS = A("DSOS", [128, NEC], F32, OFF_DSOS)
    OSC = A("OSC", [1, 1], F32, OFF_OSC)
    US = A("US", [128, NHC], F32, OFF_US)

    PS_A = nc.alloc_psum_tensor("PS_A", [128, 1024], F32)
    PS_B = nc.alloc_psum_tensor("PS_B", [128, 1024], F32)
    PS_C0 = nc.alloc_psum_tensor("PS_C0", [128, 512], F32)
    PS_C1 = nc.alloc_psum_tensor("PS_C1", [128, 512], F32)
    PS_K0 = nc.alloc_psum_tensor("PS_K0", [128, 512], F32)
    PS_K1 = nc.alloc_psum_tensor("PS_K1", [128, 512], F32)
    PB_AP = [PS_A[:, 0:512], PS_A[:, 512:1024], PS_B[:, 0:512], PS_B[:, 512:1024],
             PS_C0[:, :], PS_C1[:, :], PS_K0[:, :], PS_K1[:, :]]
    PB_H = [PS_A, PS_A, PS_B, PS_B, PS_C0, PS_C1, PS_K0, PS_K1]
    PB_OF = [0, 512, 0, 512, 0, 0, 0, 0]
    PB_RL = [1024, 1024, 1024, 1024, 512, 512, 512, 512]

    p = P(nc)
    import contextlib
    stack = contextlib.ExitStack()
    for sn in ["pe", "dve", "act", "gw", "io"]:
        p.sem(sn, stack.enter_context(nc.semaphore("sem_" + sn)))

    PB = [R() for _ in range(8)]
    R_XN = R(); R_CTX = R()
    R_A = R()
    R_XT = [R(), R()]
    R_B = [R(), R()]
    R_C0 = [R(), R()]
    R_C1 = [R(), R()]
    R_PR = [R(), R()]
    R_PT = R()
    R_MK = R()
    R_BV = R()
    R_T1 = [R(), R()]
    R_DTI = [R(), R()]
    R_CONST = R()
    R_VECLN = [R(), R()]
    R_STATS = R()

    for name, dst in [("maskc", MASK), ("identc", IDENT), ("onesb", ONESB),
                      ("onesf", ONESF), ("bq2d", BQ), ("bk2d", BK),
                      ("bp2d", BP), ("ds2d", DS), ("osc", OSC), ("us2d", US)]:
        v = p.op("y", lambda e, d=dst, s=di[name]: e.dma_start(d[:, :], s), inc=True)
    p.wr_done("io", v, R_CONST, R_MK)

    p.use("t", R_CONST)
    p.wr_acq("t", PB[7])
    v = p.op("t", lambda e: e.matmul(PB_AP[7][:, 0:1], ONESF[:, :], OSC[:, :],
                                     start=True, stop=True), inc=True)
    p.wr_done("pe", v, PB[7])
    p.use("v", PB[7]); p.use("v", R_CONST)
    vc = p.op("v", lambda e: e.tensor_copy(VEC[:, 40:41], PB_AP[7][:, 0:1]),
              inc=True)
    p.ops["v"].append(lambda e, h=p.semh["dve"], v=vc: e.wait_ge(h, v))
    p.op("v", lambda e: e.tensor_scalar(BQ[:, :], BQ[:, :], SCALE, None,
                                        op0=ALU.mult))
    v = p.op("v", lambda e: e.tensor_scalar(DSOS[:, :], DS[:, :], VEC[:, 40:41],
                                            None, op0=ALU.mult), inc=True)
    p.rd_done("dve", v, PB[7])
    p.wr_done("dve", v, R_CONST)

    if DBG:
        p.use("y", R_CONST)
        p.op("y", lambda e: e.dma_start(dbg["vecI"], VEC[:, :]), inc=True)
        p.op("y", lambda e: e.dma_start(dbg["dsosI"], DSOS[:, :]), inc=True)

    p.wr_acq("g", R_CTX)
    v = p.op("g", lambda e: e.dma_start(CTX[:, 0:8192], di["wvT"][0:128, :]),
             inc=True)
    wv_ready = {0: v}
    p.wr_done("gw", v, R_CTX)
    p.gbar()
    p.wr_acq("g", R_BV)
    v = p.op("g", lambda e: e.dma_start(BVROW[:, :], di["bvrow"][:, 0:512]), inc=True)
    bvrow_ready = {0: v}
    p.wr_done("gw", v, R_BV)
    p.gbar()
    p.wr_acq("g", R_C0[0], R_C0[1])
    p.op("g", lambda e: e.dma_start(WQK[:, 0:2048], di["wkT"][0:128, :]), inc=True)
    v = p.op("g", lambda e: e.dma_start(WQK[:, 2048:4096], di["wqT"][0:128, :]),
             inc=True)
    wqk_ready = {0: v}
    p.wr_done("gw", v, R_C0[0], R_C0[1])
    p.gbar()

    def SPC():
        v = p.op("v", lambda e: e.tensor_copy(VEC[:, 47:48], VEC[:, 40:41]),
                 inc=True)
        h = p.semh["dve"]
        p.ops["v"].append(lambda e, h=h, v=v: e.wait_ge(h, v))

    def emit_ln_chunk(s, si, dst, dst_rowlen, col_of, tr_par0):
        XTL = XTLN0 if s == 0 else XTLN1
        XNT = XNT0 if s == 0 else XNT1
        vb = s * 16
        p.use("v", R_XT[s])
        p.wr_acq("v", R_VECLN[s])
        p.op("v", lambda e, XTL=XTL, vb=vb: e.reduce_sum(
            VECL[:, vb:vb + 1], XTL[:, :], axis=AX.X))
        SPC()
        p.use("s", R_XT[s])
        p.wr_acq("s", R_C1[s])
        sq = p.op("s", lambda e, XTL=XTL, XNT=XNT, vb=vb: e.activation(
            XNT[:, :], XTL[:, :], AF.Square,
            accum_out=VECL[:, vb + 1:vb + 2]), inc=True)
        p.wait("v", "act", sq)
        p.op("v", lambda e, vb=vb: e.tensor_scalar(
            VECL[:, vb + 2:vb + 3], VECL[:, vb:vb + 1], 1.0 / DM, None, op0=ALU.mult))
        p.op("v", lambda e, vb=vb: e.tensor_scalar(
            VECL[:, vb + 3:vb + 4], VECL[:, vb + 1:vb + 2], 1.0 / DM, EPS,
            op0=ALU.mult, op1=ALU.add))
        SPC()
        p.op("v", lambda e, vb=vb: e.tensor_tensor(
            VECL[:, vb + 7:vb + 8], VECL[:, vb + 2:vb + 3], VECL[:, vb + 2:vb + 3], op=ALU.mult))
        SPC()
        vv = p.op("v", lambda e, vb=vb: e.tensor_tensor(
            VECL[:, vb + 3:vb + 4], VECL[:, vb + 3:vb + 4], VECL[:, vb + 7:vb + 8],
            op=ALU.subtract), inc=True)
        p.wait("s", "dve", vv)
        sv = p.op("s", lambda e, vb=vb: e.activation(
            VECL[:, vb + 4:vb + 5], VECL[:, vb + 3:vb + 4], AF.Sqrt), inc=True)
        p.wait("v", "act", sv)
        p.op("v", lambda e, vb=vb: e.reciprocal(
            VECL[:, vb + 5:vb + 6], VECL[:, vb + 4:vb + 5]))
        SPC()
        vv = p.op("v", lambda e, vb=vb: e.tensor_scalar(
            VECL[:, vb + 6:vb + 7], VECL[:, vb + 2:vb + 3], -1.0,
            VECL[:, vb + 5:vb + 6], op0=ALU.mult, op1=ALU.mult), inc=True)
        p.wr_done("dve", vv, R_VECLN[s])
        p.wait("s", "dve", vv)
        sv = p.op("s", lambda e, XTL=XTL, XNT=XNT, vb=vb: e.activation(
            XNT[:, :], XTL[:, :], AF.Identity,
            bias=VECL[:, vb + 6:vb + 7], scale=VECL[:, vb + 5:vb + 6]), inc=True)
        p.rd_done("act", sv, R_XT[s])
        p.rd_done("act", sv, R_VECLN[s])
        p.wr_done("act", sv, R_C1[s])
        SPC()
        cv = p.op("v", lambda e, si=si, vb=vb: e.tensor_copy(
            STATS[:, 2 * si:2 * si + 2], VECL[:, vb + 5:vb + 7]), inc=True)
        p.wr_done("dve", cv, R_STATS)
        return emit_xn_transposes(XNT, s, dst, dst_rowlen, col_of, tr_par0)

    def emit_xn_transposes(XNT, s, dst, dst_rowlen, col_of, tr_par0):
        p.use("t", R_C1[s], R_CONST)
        last = None
        for w in range(4):
            bi = 4 + (tr_par0 + w) % 2
            p.wr_acq("t", PB[bi])
            for j in range(4):
                c = w * 4 + j
                iv = p.op("t", lambda e, c=c, j=j, bi=bi, XNT=XNT: e.matmul(
                    PB_AP[bi][:, j * 128:(j + 1) * 128],
                    XNT[:, c * 128:(c + 1) * 128], IDENT[:, :],
                    start=True, stop=True), inc=(j == 3))
            p.wr_done("pe", iv, PB[bi])
            p.rd_done("pe", iv, R_C1[s])
            p.use("v", PB[bi])
            c0 = w * 4
            dap = ap3(dst, dst_rowlen, col_of(c0), col_of(c0 + 1) - col_of(c0), 4, 128)
            sap = ap3(PB_H[bi], PB_RL[bi], PB_OF[bi], 128, 4, 128)
            last = p.op("v", lambda e, dap=dap, sap=sap: e.tensor_copy(dap, sap), inc=True)
            p.rd_done("dve", last, PB[bi])
        return last

    def emit_lnC_chunk(s, si, dst, dst_rowlen, col_of, tr_par0):
        XTL = XTLN0 if s == 0 else XTLN1
        XNT = XNT0 if s == 0 else XNT1
        p.use("s", R_XT[s], R_STATS)
        p.wr_acq("s", R_C1[s])
        sv = p.op("s", lambda e, XTL=XTL, XNT=XNT, si=si: e.activation(
            XNT[:, :], XTL[:, :], AF.Identity,
            bias=STATS[:, 2 * si + 1:2 * si + 2],
            scale=STATS[:, 2 * si:2 * si + 1]), inc=True)
        p.rd_done("act", sv, R_XT[s])
        p.rd_done("act", sv, R_STATS)
        p.wr_done("act", sv, R_C1[s])
        return emit_xn_transposes(XNT, s, dst, dst_rowlen, col_of, tr_par0)

    for sc in range(16):
        s = sc % 2
        XTL = XTLN0 if s == 0 else XTLN1
        p.wr_acq("y", R_XT[s])
        iv = p.op("y", lambda e, sc=sc, XTL=XTL: e.dma_start(
            XTL[:, :], di["x"][sc * 128:(sc + 1) * 128, :]), inc=True)
        p.wr_done("io", iv, R_XT[s])
        if sc == 0:
            p.wr_acq("v", R_XN)
        lv = emit_ln_chunk(s, sc, XN, 32768, lambda c, sc=sc: c * 2048 + sc * 128, sc * 4)
    p.wr_done("dve", lv, R_XN)

    if DBG:
        p.use("y", R_XN)
        p.op("y", lambda e: e.dma_start(dbg["xn"], XN[:, :]), inc=True)
        v = p.op("y", lambda e: e.dma_start(dbg["vecA"], VEC[:, :]), inc=True)
        p.rd_done("io", v, R_VECLN[0], R_VECLN[1])

    qrot = [0]
    exp_after = {}
    d_after = {}

    def emit_v4(g):
        p.use("t", R_BV, R_CONST)
        p.wr_acq("t", PB[6])
        v = p.op("t", lambda e: e.matmul(PB_AP[6], ONESB[:, :], BVROW[:, :],
                                         start=True, stop=True), inc=True)
        p.wr_done("pe", v, PB[6])
        p.rd_done("pe", v, R_BV)
        p.use("s", PB[6])
        p.wr_acq("s", R_BV)
        sv = p.op("s", lambda e: e.activation(BVBC[:, :], PB_AP[6], AF.Identity), inc=True)
        p.rd_done("act", sv, PB[6])
        p.wr_done("act", sv, R_BV)
        p.wait("t", "gw", wv_ready[g])
        p.use("t", R_XN, R_CTX)
        for sc in range(16):
            bi = 6 + (sc % 2)
            p.wr_acq("t", PB[bi])
            for c in range(NDC):
                v = p.op("t", lambda e, c=c, sc=sc, g=g, bi=bi: e.matmul(
                    PB_AP[bi],
                    XN[:, c * 2048 + sc * 128: c * 2048 + (sc + 1) * 128],
                    CTX[:, g * 8192 + c * 512: g * 8192 + (c + 1) * 512],
                    start=(c == 0), stop=(c == NDC - 1)), inc=(c == NDC - 1))
            p.wr_done("pe", v, PB[bi])
            p.rd_done("pe", v, R_CTX, R_XN)
            p.use("s", PB[bi])
            if sc == 0:
                p.wr_acq("s", R_A, R_XT[0], R_XT[1])
            sv = p.op("s", lambda e, sc=sc, bi=bi: e.activation(
                V4[:, sc * 512:(sc + 1) * 512], PB_AP[bi], AF.Identity), inc=True)
            p.rd_done("act", sv, PB[bi])
            p.use("v", R_BV)
            vv = p.op("v", lambda e, sc=sc: e.tensor_tensor(
                V4[:, sc * 512:(sc + 1) * 512], V4[:, sc * 512:(sc + 1) * 512],
                BVBC[:, :], op=ALU.add), inc=True)
            p.rd_done("dve", vv, R_BV)
        p.wr_done("dve", vv, R_A, R_XT[0], R_XT[1])

    def emit_kq(h):
        par = h % 2
        p.wait("t", "gw", wqk_ready[h])
        p.use("t", R_XN)
        wqk_res = (R_C0[0], R_C0[1]) if par == 0 else (R_C1[0], R_C1[1])
        for qk in range(2):
            for sg in range(4):
                bi = 6 + ((qk * 4 + sg) % 2)
                p.wr_acq("t", PB[bi])
                for c in range(NDC):
                    wcol = par * 4096 + qk * 2048 + c * 128
                    v = p.op("t", lambda e, c=c, sg=sg, wcol=wcol, bi=bi: e.matmul(
                        PB_AP[bi], WQK[:, wcol:wcol + 128],
                        XN[:, c * 2048 + sg * 512: c * 2048 + (sg + 1) * 512],
                        start=(c == 0), stop=(c == NDC - 1)), inc=(c == NDC - 1))
                p.wr_done("pe", v, PB[bi])
                p.rd_done("pe", v, R_XN, *wqk_res)
                dcol = par * 4096 + qk * 2048 + sg * 512
                if qk == 0:
                    p.use("s", PB[bi], R_CONST)
                    if sg == 0:
                        p.wr_acq("s", R_B[par])
                    sv = p.op("s", lambda e, dcol=dcol, h=h, bi=bi: e.activation(
                        KQ[:, dcol:dcol + 512], PB_AP[bi], AF.Identity,
                        bias=BK[:, h:h + 1]), inc=True)
                    p.rd_done("act", sv, PB[bi])
                    last_eng, last_v = "act", sv
                else:
                    p.use("v", PB[bi], R_CONST)
                    p.wait("v", "act", p.cnt("act"))
                    sv = p.op("v", lambda e, dcol=dcol, h=h, bi=bi: e.tensor_scalar(
                        KQ[:, dcol:dcol + 512], PB_AP[bi], SCALE, BQ[:, h:h + 1],
                        op0=ALU.mult, op1=ALU.add), inc=True)
                    p.rd_done("dve", sv, PB[bi])
                    last_eng, last_v = "dve", sv
        p.wr_done(last_eng, last_v, R_B[par])

    def rd_wqk(par, v):
        if par == 0:
            p.rd_done("pe", v, R_C0[0], R_C0[1])
        else:
            p.rd_done("pe", v, R_C1[0], R_C1[1])

    def emit_scores(t, h):
        par = t % 2
        vb = par * 16
        qg = t // 2
        chunks = 2 * qg + 2
        nkt = (chunks + 3) // 4
        off = (t % 4) * 128
        kcol = (h % 2) * 4096
        qcol = kcol + 2048
        p.use("t", R_B[h % 2])
        for kt in range(nkt):
            bi = qrot[0] % 4
            qrot[0] += 1
            p.wr_acq("t", PB[bi])
            v = p.op("t", lambda e, t=t, kt=kt, bi=bi: e.matmul(
                PB_AP[bi], KQ[:, qcol + t * 128: qcol + (t + 1) * 128],
                KQ[:, kcol + kt * 512: kcol + (kt + 1) * 512],
                start=True, stop=True), inc=True)
            p.wr_done("pe", v, PB[bi])
            p.rd_done("pe", v, R_B[h % 2])
            if kt == nkt - 1:
                p.use("v", PB[bi])
                p.use("v", R_MK)
                vv = p.op("v", lambda e, bi=bi, off=off: e.tensor_tensor(
                    PB_AP[bi][:, off:512], PB_AP[bi][:, off:512],
                    MASK[:, 0:512 - off], op=ALU.add), inc=True)
                p.rd_done("dve", vv, R_MK)
                p.wait("s", "dve", vv)
            p.use("s", PB[bi])
            if kt == 0:
                p.wr_acq("s", R_PR[par], R_DTI[par])
            sv = p.op("s", lambda e, bi=bi, par=par, kt=kt, a=vb + 4 + kt: e.activation(
                PROBS[:, par * 2048 + kt * 512: par * 2048 + (kt + 1) * 512],
                PB_AP[bi], AF.Exp,
                accum_out=VEC[:, a:a + 1]), inc=True)
            p.rd_done("act", sv, PB[bi])
        p.wr_done("act", sv, R_PR[par], R_DTI[par])
        exp_after[t] = sv
        p.wait("v", "act", sv)
        if nkt > 1:
            p.op("v", lambda e, vb=vb, nkt=nkt: e.reduce_sum(
                VEC[:, vb + 13:vb + 14], VEC[:, vb + 4:vb + 4 + nkt], axis=AX.X))
            SPC()
            p.op("v", lambda e, vb=vb: e.reciprocal(VEC[:, vb + 14:vb + 15],
                                                    VEC[:, vb + 13:vb + 14]))
        else:
            p.op("v", lambda e, vb=vb: e.reciprocal(VEC[:, vb + 14:vb + 15],
                                                    VEC[:, vb + 4:vb + 5]))
        p.use("v", R_CONST)
        SPC()
        dv = p.op("v", lambda e, par=par, vb=vb: e.tensor_scalar(
            DTI[:, par * 512: par * 512 + 128], IDENT[:, :],
            VEC[:, vb + 14:vb + 15], None, op0=ALU.mult), inc=True)
        p.wr_done("dve", dv, R_DTI[par])
        d_after[t] = dv

    def emit_transposes(t):
        par = t % 2
        qg = t // 2
        chunks = 2 * qg + 2
        p.wait("t", "dve", d_after[t])
        p.use("t", R_PR[par], R_DTI[par])
        for b0 in range(0, chunks, 4):
            nb = min(4, chunks - b0)
            bi = 4 + (b0 // 4) % 2
            p.wr_acq("t", PB[bi])
            for i in range(nb):
                kc = b0 + i
                v = p.op("t", lambda e, kc=kc, i=i, bi=bi, par=par: e.matmul(
                    PB_AP[bi][:, i * 128:(i + 1) * 128],
                    PROBS[:, par * 2048 + kc * 128: par * 2048 + (kc + 1) * 128],
                    DTI[:, par * 512: par * 512 + 128],
                    start=True, stop=True), inc=(i == nb - 1))
            p.wr_done("pe", v, PB[bi])
            p.rd_done("pe", v, R_PR[par], R_DTI[par])
            p.use("v", PB[bi])
            if b0 == 0:
                p.wr_acq("v", R_PT)
            dap = ap3(PROBST, 4096, b0 * 256 + par * 128, 256, nb, 128)
            sap = ap3(PB_H[bi], PB_RL[bi], PB_OF[bi], 128, nb, 128)
            vv = p.op("v", lambda e, dap=dap, sap=sap: e.tensor_copy(dap, sap), inc=True)
            p.rd_done("dve", vv, PB[bi])
        p.wr_done("dve", vv, R_PT)

    def emit_pv(qg, h):
        j = h % 4
        chunks = 2 * qg + 2
        p.use("t", R_PT, R_A, R_XT[0], R_XT[1])
        p.wr_acq("t", PB[6])
        for kc in range(chunks):
            v = p.op("t", lambda e, kc=kc, j=j: e.matmul(
                PB_AP[6][:, 0:256],
                V4[:, kc * 512 + j * 128: kc * 512 + (j + 1) * 128],
                PROBST[:, kc * 256:(kc + 1) * 256],
                start=(kc == 0), stop=(kc == chunks - 1)), inc=(kc == chunks - 1))
        p.wr_done("pe", v, PB[6])
        p.rd_done("pe", v, R_PT, R_A, R_XT[0], R_XT[1])
        p.use("v", PB[6])
        vv = p.op("v", lambda e, qg=qg, h=h: e.tensor_copy(
            CTX[:, h * 2048 + qg * 256: h * 2048 + (qg + 1) * 256],
            PB_AP[6][:, 0:256]), inc=True)
        p.rd_done("dve", vv, PB[6])
        p.wr_done("dve", vv, R_CTX)

    for g in range(4):
        emit_v4(g)
        if g < 3:
            v = p.op("g", lambda e, g=g: e.dma_start(
                CTX[:, (g + 1) * 8192:(g + 2) * 8192],
                di["wvT"][(g + 1) * 128:(g + 2) * 128, :]), inc=True)
            wv_ready[g + 1] = v
            p.wr_acq("g", R_BV)
            v = p.op("g", lambda e, g=g: e.dma_start(
                BVROW[:, :], di["bvrow"][:, (g + 1) * 512:(g + 2) * 512]), inc=True)
            bvrow_ready[g + 1] = v
            p.wr_done("gw", v, R_BV)
            p.gbar()
        for j4 in range(4):
            h = 4 * g + j4
            emit_kq(h)
            if h < NH - 1:
                parn = (h + 1) % 2
                rs = (R_C0[0], R_C0[1]) if parn == 0 else (R_C1[0], R_C1[1])
                p.wr_acq("g", *rs)
                p.op("g", lambda e, h=h, parn=parn: e.dma_start(
                    WQK[:, parn * 4096:parn * 4096 + 2048],
                    di["wkT"][(h + 1) * 128:(h + 2) * 128, :]), inc=True)
                v = p.op("g", lambda e, h=h, parn=parn: e.dma_start(
                    WQK[:, parn * 4096 + 2048:parn * 4096 + 4096],
                    di["wqT"][(h + 1) * 128:(h + 2) * 128, :]), inc=True)
                wqk_ready[h + 1] = v
                p.wr_done("gw", v, *rs)
                p.gbar()
            for t in range(16):
                if t >= 2:
                    emit_transposes(t - 2)
                if t >= 3 and t % 2 == 1:
                    emit_pv((t - 3) // 2, h)
                emit_scores(t, h)
            emit_transposes(14)
            emit_transposes(15)
            emit_pv(7, h)

    if DBG:
        p.use("y", R_CTX, R_A, R_B[0], R_B[1], R_PT)
        p.use("y", R_PR[0], R_PR[1], R_DTI[0], R_DTI[1])
        p.op("y", lambda e: e.dma_start(dbg["ctx"], CTX[:, :]), inc=True)
        p.op("y", lambda e: e.dma_start(dbg["kqv"][:, 0:8192], V4[:, :]), inc=True)
        p.op("y", lambda e: e.dma_start(dbg["kqv"][:, 8192:16384], KQ[:, :]), inc=True)
        p.op("y", lambda e: e.dma_start(dbg["pt"], PROBST[:, :]), inc=True)
        p.op("y", lambda e: e.dma_start(dbg["vecB"], VEC[:, :]), inc=True)
        p.op("y", lambda e: e.dma_start(dbg["dti"], DTI[:, :]), inc=True)
        v = p.op("y", lambda e: e.dma_start(dbg["probs"], PROBS[:, :]), inc=True)
        p.rd_done("io", v, R_CTX, R_A, R_B[0], R_B[1], R_PT)
        p.rd_done("io", v, R_PR[0], R_PR[1], R_DTI[0], R_DTI[1], R_VECLN[0], R_VECLN[1])

    for ps in range(NPASS):
        s0 = ps * SP
        first = ps == 0
        for sc4 in range(4):
            s = sc4 % 2
            XTL = XTLN0 if s == 0 else XTLN1
            p.wr_acq("y", R_XT[s])
            iv = p.op("y", lambda e, s0=s0, sc4=sc4, XTL=XTL: e.dma_start(
                XTL[:, :], di["x"][s0 + sc4 * 128: s0 + (sc4 + 1) * 128, :]), inc=True)
            p.wr_done("io", iv, R_XT[s])
            if sc4 == 0:
                p.wr_acq("v", R_PR[0], R_PR[1], R_PT)
            lv = emit_ln_chunk(s, ps * 4 + sc4, XNS, 8192,
                               lambda c, sc4=sc4: c * 512 + sc4 * 128, sc4 * 4)
        p.wr_done("dve", lv, R_PR[0], R_PR[1], R_PT)
        R_XNS = (R_PR[0], R_PR[1], R_PT)
        if DBG and ps == 0:
            p.use("y", *R_XNS)
            v = p.op("y", lambda e: e.dma_start(dbg["hid"][:, 0:8192], XNS[:, :]), inc=True)
            p.rd_done("io", v, *R_XNS)

        for hc in range(NHC):
            par = hc % 2
            ucol = par * 2048
            if first:
                p.wr_acq("g", R_C0[par])
                v = p.op("g", lambda e, ucol=ucol, hc=hc: e.dma_start(
                    UPW[:, ucol:ucol + 2048],
                    di["upT"][hc * 128:(hc + 1) * 128, :]), inc=True)
                p.wr_done("gw", v, R_C0[par])
                p.gbar()
                p.use("s", R_C0[par])
                sv = p.op("s", lambda e, ucol=ucol: e.sign(
                    UPW[:, ucol:ucol + 2048], UPW[:, ucol:ucol + 2048]), inc=True)
                p.wr_done("act", sv, R_C0[par])
                p.use("y", R_C0[par])
                yv = p.op("y", lambda e, ucol=ucol, hc=hc: e.dma_start(
                    up_s[hc * 128:(hc + 1) * 128, :], UPW[:, ucol:ucol + 2048]), inc=True)
                p.rd_done("io", yv, R_C0[par])
            else:
                p.wr_acq("y", R_C0[par])
                v = p.op("y", lambda e, ucol=ucol, hc=hc: e.dma_start(
                    UPW[:, ucol:ucol + 2048], up_s[hc * 128:(hc + 1) * 128, :]), inc=True)
                p.wr_done("io", v, R_C0[par])
            bi = 6 + (hc % 2)
            p.use("t", R_C0[par], *R_XNS)
            p.wr_acq("t", PB[bi])
            for c in range(NDC):
                v = p.op("t", lambda e, c=c, ucol=ucol, bi=bi: e.matmul(
                    PB_AP[bi], UPW[:, ucol + c * 128: ucol + (c + 1) * 128],
                    XNS[:, c * 512:(c + 1) * 512],
                    start=(c == 0), stop=(c == NDC - 1)), inc=(c == NDC - 1))
            p.wr_done("pe", v, PB[bi])
            p.rd_done("pe", v, R_C0[par], *R_XNS)
            p.use("s", PB[bi], R_CONST)
            if hc == 0:
                p.wr_acq("s", R_XN)
            sv = p.op("s", lambda e, hc=hc, bi=bi: e.activation(
                XN[:, hc * 512:(hc + 1) * 512], PB_AP[bi], AF.Relu,
                scale=US[:, hc:hc + 1]), inc=True)
            p.rd_done("act", sv, PB[bi])
        p.wr_done("act", sv, R_XN)
        if DBG and ps == 0:
            p.use("y", R_XN)
            v = p.op("y", lambda e: e.dma_start(dbg["hid"][:, 8192:16384], XN[:, 0:8192]), inc=True)
            p.rd_done("io", v, R_XN)

        for ec in range(NEC):
            par = ec % 2
            dcol = par * 8192
            pcol = par * 2048
            pair = ec % 4
            bp_, bf_ = 2 * pair, 2 * pair + 1
            dn_res = (R_A, R_XT[0], R_XT[1]) if par == 0 else (R_B[0], R_B[1])
            if first:
                p.wr_acq("g", *dn_res)
                v = p.op("g", lambda e, dcol=dcol, ec=ec: e.dma_start(
                    DNW[:, dcol:dcol + 8192],
                    di["dnT"][ec * 128:(ec + 1) * 128, :]), inc=True)
                p.wr_done("gw", v, *dn_res)
                p.gbar()
                p.use("s", *dn_res)
                sv = p.op("s", lambda e, dcol=dcol: e.sign(
                    DNW[:, dcol:dcol + 8192], DNW[:, dcol:dcol + 8192]), inc=True)
                p.wr_done("act", sv, *dn_res)
                p.use("y", *dn_res)
                yv = p.op("y", lambda e, dcol=dcol, ec=ec: e.dma_start(
                    dn_s[ec * 128:(ec + 1) * 128, :], DNW[:, dcol:dcol + 8192]), inc=True)
                p.rd_done("io", yv, *dn_res)
                p.wr_acq("g", R_C1[par])
                v = p.op("g", lambda e, pcol=pcol, ec=ec: e.dma_start(
                    PJW[:, pcol:pcol + 2048],
                    di["wpT"][ec * 128:(ec + 1) * 128, :]), inc=True)
                p.wr_done("gw", v, R_C1[par])
                p.gbar()
                p.use("y", R_C1[par])
                yv = p.op("y", lambda e, pcol=pcol, ec=ec: e.dma_start(
                    pj_s[ec * 128:(ec + 1) * 128, :], PJW[:, pcol:pcol + 2048]), inc=True)
                p.rd_done("io", yv, R_C1[par])
            else:
                p.wr_acq("y", *dn_res)
                v = p.op("y", lambda e, dcol=dcol, ec=ec: e.dma_start(
                    DNW[:, dcol:dcol + 8192], dn_s[ec * 128:(ec + 1) * 128, :]), inc=True)
                p.wr_done("io", v, *dn_res)
                p.wr_acq("y", R_C1[par])
                v = p.op("y", lambda e, pcol=pcol, ec=ec: e.dma_start(
                    PJW[:, pcol:pcol + 2048], pj_s[ec * 128:(ec + 1) * 128, :]), inc=True)
                p.wr_done("io", v, R_C1[par])
            xres_t = XRES0 if par == 0 else XRES1
            xres_r = R_MK if par == 0 else R_BV
            p.wr_acq("y", xres_r)
            xv = p.op("y", lambda e, ec=ec, s0=s0, xres_t=xres_t: e.dma_start(
                xres_t[:, :], di["xT"][ec * 128:(ec + 1) * 128, s0:s0 + 512]), inc=True)
            p.wr_done("io", xv, xres_r)
            p.use("t", R_C1[par], R_CTX)
            p.wr_acq("t", PB[bp_])
            for c in range(NDC):
                v = p.op("t", lambda e, c=c, pcol=pcol, bp_=bp_, s0=s0: e.matmul(
                    PB_AP[bp_], PJW[:, pcol + c * 128: pcol + (c + 1) * 128],
                    CTX[:, c * 2048 + s0: c * 2048 + s0 + 512],
                    start=(c == 0), stop=(c == NDC - 1)), inc=(c == NDC - 1))
            p.wr_done("pe", v, PB[bp_])
            p.rd_done("pe", v, R_C1[par], R_CTX)
            p.use("t", *dn_res)
            p.use("t", R_XN)
            p.wr_acq("t", PB[bf_])
            for hc in range(NHC):
                v = p.op("t", lambda e, hc=hc, dcol=dcol, bf_=bf_: e.matmul(
                    PB_AP[bf_], DNW[:, dcol + hc * 128: dcol + (hc + 1) * 128],
                    XN[:, hc * 512:(hc + 1) * 512],
                    start=(hc == 0), stop=(hc == NHC - 1)), inc=(hc == NHC - 1))
            p.wr_done("pe", v, PB[bf_])
            p.rd_done("pe", v, R_XN)
            p.rd_done("pe", v, *dn_res)
            tp = ec % 2
            t1 = T1_0 if tp == 0 else T1_1
            p.use("s", PB[bf_], R_CONST)
            p.wr_acq("s", R_T1[tp])
            sv = p.op("s", lambda e, bf_=bf_, ec=ec, t1=t1: e.activation(
                t1[:, :], PB_AP[bf_], AF.Identity, scale=DSOS[:, ec:ec + 1],
                bias=BP[:, ec:ec + 1]), inc=True)
            p.rd_done("act", sv, PB[bf_])
            p.wr_done("act", sv, R_T1[tp])
            p.use("v", PB[bp_], R_T1[tp], R_CONST)
            p.op("v", lambda e, bp_=bp_, t1=t1: e.tensor_tensor(
                t1[:, :], PB_AP[bp_], t1[:, :], op=ALU.add))
            p.use("v", xres_r)
            SPC()
            vv = p.op("v", lambda e, t1=t1, xres_t=xres_t: e.tensor_tensor(
                t1[:, :], t1[:, :], xres_t[:, :], op=ALU.add), inc=True)
            p.rd_done("dve", vv, PB[bp_], xres_r)
            p.wr_done("dve", vv, R_T1[tp])
            p.use("y", R_T1[tp])
            ov = p.op("y", lambda e, ec=ec, s0=s0, t1=t1: e.dma_start(
                y[ec * 128:(ec + 1) * 128, s0:s0 + 512], t1[:, :]), inc=True)
            p.rd_done("io", ov, R_T1[tp])

    if DBG:
        p.use("y", R_CONST)
        p.op("y", lambda e: e.dma_start(dbg["misc"][:, 0:16], DSOS[:, :]), inc=True)
        p.op("y", lambda e: e.dma_start(dbg["misc"][:, 16:32], DS[:, :]), inc=True)
        p.op("y", lambda e: e.dma_start(dbg["misc"][:, 32:48], BP[:, :]), inc=True)
        p.op("y", lambda e: e.dma_start(dbg["misc"][:, 48:64], BQ[:, :]), inc=True)
        p.op("y", lambda e: e.dma_start(dbg["misc"][:, 64:80], US[:, 0:16]), inc=True)
        p.use("y", R_A, R_B[0], R_B[1])
        p.op("y", lambda e: e.dma_start(dbg["dnw"], DNW[:, 0:8192]), inc=True)
        p.use("y", R_XN)
        p.op("y", lambda e: e.dma_start(dbg["hid2"], XN[:, :]), inc=True)
        p.use("y", R_T1[0], R_T1[1])
        p.op("y", lambda e: e.dma_start(dbg["t1"][:, 0:512], T1_0[:, :]), inc=True)
        p.op("y", lambda e: e.dma_start(dbg["t1"][:, 512:1024], T1_1[:, :]), inc=True)

    p.wait("y", "io", p.cnt("io"))
    p.wait("g", "gw", p.cnt("gw"))

    with nc.Block() as block:
        @block.tensor
        def _(t):
            for f in p.ops["t"]:
                f(t)

        @block.vector
        def _(v):
            for f in p.ops["v"]:
                f(v)

        @block.scalar
        def _(s):
            for f in p.ops["s"]:
                f(s)

        @block.gpsimd
        def _(g):
            for f in p.ops["g"]:
                f(g)

        @block.sync
        def _(sy):
            for f in p.ops["y"]:
                f(sy)
    stack.close()
    return nc


def kernel(L0_attn_x, L0_attn_w_attn, L0_attn_b_attn, L0_attn_w_proj,
           L0_attn_b_proj, L0_ffn_up_weight, L0_ffn_down_weight,
           L0_ffn_up_scale, L0_ffn_down_scale, L0_ffn_output_scale):
    x = np.ascontiguousarray(np.asarray(L0_attn_x, dtype=np.float32))
    w_attn = np.asarray(L0_attn_w_attn, dtype=np.float32)
    b_attn = np.asarray(L0_attn_b_attn, dtype=np.float32)
    w_proj = np.asarray(L0_attn_w_proj, dtype=np.float32)
    b_proj = np.asarray(L0_attn_b_proj, dtype=np.float32)
    up_w = np.asarray(L0_ffn_up_weight, dtype=np.float32)
    dn_w = np.asarray(L0_ffn_down_weight, dtype=np.float32)
    us = np.asarray(L0_ffn_up_scale, dtype=np.float32)
    ds = np.asarray(L0_ffn_down_scale, dtype=np.float32)
    osc = np.asarray(L0_ffn_output_scale, dtype=np.float32)

    wsp = w_attn.reshape(NH, 3, HD, DM)
    wqT = np.ascontiguousarray(
        wsp[:, 0].reshape(NH, HD, NDC, 128).transpose(0, 3, 2, 1).reshape(DM, DM))
    wkT = np.ascontiguousarray(
        wsp[:, 1].reshape(NH, HD, NDC, 128).transpose(0, 3, 2, 1).reshape(DM, DM))
    wvT = np.ascontiguousarray(
        wsp[:, 2].reshape(4, 512, NDC, 128).transpose(0, 3, 2, 1).reshape(512, DH))
    bsp = b_attn.reshape(NH, 3, HD)
    bq2d = np.ascontiguousarray(bsp[:, 0].reshape(NH, HD).T)
    bk2d = np.ascontiguousarray(bsp[:, 1].reshape(NH, HD).T)
    bvrow = np.ascontiguousarray(bsp[:, 2].reshape(1, DM))
    wpT = np.ascontiguousarray(
        w_proj.reshape(NEC, 128, NDC, 128).transpose(0, 3, 2, 1).reshape(DM, DM))
    bp2d = np.ascontiguousarray(b_proj.reshape(NEC, 128).T)
    upT = np.ascontiguousarray(
        up_w.reshape(NHC, 128, NDC, 128).transpose(0, 3, 2, 1).reshape(DH, DM))
    dnT = np.ascontiguousarray(
        dn_w.reshape(NEC, 128, NHC, 128).transpose(0, 3, 2, 1).reshape(DM, DH))
    us2d = np.ascontiguousarray(us.reshape(NHC, 128).T)
    ds2d = np.ascontiguousarray(ds.reshape(NEC, 128).T)
    osc2d = np.ascontiguousarray(osc.reshape(1, 1))

    mask = np.zeros((128, 512), dtype=np.float32)
    for i in range(128):
        mask[i, i + 1:] = NEG
    ident = np.eye(128, dtype=ml_dtypes.bfloat16)
    onesb = np.ones((1, 128), dtype=ml_dtypes.bfloat16)
    onesf = np.ones((1, 128), dtype=np.float32)

    shared = dict(wqT=wqT, wkT=wkT, wvT=wvT, bq2d=bq2d, bk2d=bk2d, bvrow=bvrow,
                  wpT=wpT, bp2d=bp2d, upT=upT, dnT=dnT, us2d=us2d, ds2d=ds2d,
                  osc=osc2d, maskc=mask, identc=ident, onesb=onesb, onesf=onesf)
    in_maps = []
    for b in range(8):
        m = dict(shared)
        m["x"] = np.ascontiguousarray(x[b])
        m["xT"] = np.ascontiguousarray(x[b].T)
        in_maps.append(m)

    nc = build_nc()
    res = run_bass_kernel_spmd(nc, in_maps, core_ids=list(range(8)))
    if DBG:
        kernel._dbg = [dict(r) for r in (res.results[i] for i in range(8))]
    out = np.stack([np.asarray(r["y"]).T for r in res.results], axis=0)
    return np.ascontiguousarray(out.astype(np.float32))
